# version 02 (final)
# baseline (speedup 1.0000x reference)
import contextlib
import numpy as np
import ml_dtypes

import concourse.bass as bass
import concourse.mybir as mybir
from concourse.bass_utils import run_bass_kernel_spmd

F32 = mybir.dt.float32
BF16 = mybir.dt.bfloat16
BF = ml_dtypes.bfloat16
ALU = mybir.AluOpType
AF = mybir.ActivationFunctionType
AX = mybir.AxisListType

T = 2048
D = 1024
NT = T // 128
NF = D // 128

OFFSETS = sorted(set(range(33)) | {48, 64, 96, 128, 192, 256, 384, 512, 768, 1024, 1536})
NOFF = len(OFFSETS)
HA = 16
INV_A = 0.125
NEG = -1e30

FH = 16; FHD = 64; FL = 64; FE = 8
MH = 32; MP = 64; MN = 128; MK = 4
M_INTER = 2048; M_CONVD = 2304; M_PROJ = 4384
MCH = 128; MNC = T // MCH

DBG = []
STAGE = 9
TRUNC = None
TE_DRAIN = False


_DTSZ = {F32: 4, BF16: 2}


def _ap_key(ap):
    t = ap.tensor
    dims = list(ap.ap)
    try:
        import concourse.bass_types as _bt
        sp = str(ap.space)
        sb = sp in ("MemorySpace.SBUF", "MemorySpace.PSUM")
        ispsum = sp == "MemorySpace.PSUM"
    except Exception:
        sb = True
        ispsum = False
    isz = _DTSZ.get(t.dtype, 4)
    off = int(ap.offset)
    if sb:
        rs = int(dims[0][0])
        psz = ap.partition_size()
        if rs > 0:
            plo = off // rs
            off = off % rs
        else:
            plo = 0
        dims = dims[1:]
    else:
        plo, psz = 0, 1
    fwd = 0
    back = 0
    for st, ct in dims:
        if ct > 1:
            d = int(st) * (int(ct) - 1)
            if d >= 0:
                fwd += d
            else:
                back -= d
    lo = (off - back) * isz
    hi = (off + fwd + 1) * isz
    if ispsum:
        lo = (lo // 2048) * 2048
        hi = ((hi + 2047) // 2048) * 2048
    return (t.name, plo, plo + psz, lo, hi)


def _overlap(a, b):
    return (a[1] < b[2] and b[1] < a[2] and a[3] < b[4] and b[3] < a[4])


class Op:
    __slots__ = ("eng", "fn", "ins", "outs", "seq", "kind", "needed",
                 "inc_val", "waits", "drain_before", "gid")

    def __init__(self, eng, fn, ins, outs, seq, kind, gid):
        self.eng = eng
        self.fn = fn
        self.ins = ins
        self.outs = outs
        self.seq = seq
        self.kind = kind
        self.needed = False
        self.inc_val = None
        self.waits = []
        self.drain_before = False
        self.gid = gid


class Prog:
    ENGINES = ("sync", "tensor", "vector", "scalar", "gpsimd")

    def __init__(self, nc):
        self.nc = nc
        self.es = contextlib.ExitStack()
        self.ops = {e: [] for e in self.ENGINES}
        self._psn = 0
        self._gid = 0
        self.psum_names = set()
        self.hist = {}
        self.dep_map = {}
        self.all_ops = []

    def sbuf(self, name, shape, dtype):
        return self.es.enter_context(self.nc.sbuf_tensor(name, list(shape), dtype))

    def psum(self, shape, dtype=F32, name=None):
        if name is None:
            name = f"ps{self._psn}"
            self._psn += 1
        self.psum_names.add(name)
        return self.es.enter_context(self.nc.psum_tensor(name, list(shape), dtype))

    def _record(self, eng, fn, ins, outs, kind, gid=None):
        if gid is None:
            self._gid += 1
            gid = self._gid
        in_keys = [_ap_key(a) for a in ins]
        out_keys = [_ap_key(a) for a in outs]
        op = Op(eng, fn, in_keys, out_keys, len(self.ops[eng]), kind, gid)
        prods = set()
        for k in in_keys:
            for (hk, hw, hop) in self.hist.get(k[0], ()):
                if hw and _overlap(k, hk):
                    prods.add(hop)
        for k in out_keys:
            for (hk, hw, hop) in self.hist.get(k[0], ()):
                if _overlap(k, hk):
                    prods.add(hop)
        prods.discard(op)
        if prods:
            self.dep_map[id(op)] = prods
        self.ops[eng].append(op)
        self.all_ops.append(op)
        for k in in_keys:
            self.hist.setdefault(k[0], []).append((k, False, op))
        for k in out_keys:
            self.hist.setdefault(k[0], []).append((k, True, op))
        return op

    def op(self, eng, fn, ins=(), outs=()):
        return self._record(eng, fn, list(ins), list(outs), "op")

    def dma(self, eng, out_ap, in_ap):
        return self._record(eng, lambda e, o=out_ap, i=in_ap: e.dma_start(out=o, in_=i),
                            [in_ap], [out_ap], "dma")

    def dma_t(self, eng, out_ap, in_ap):
        return self._record(eng, lambda e, o=out_ap, i=in_ap: e.dma_start(
            out=o, in_=i, transpose=True), [in_ap], [out_ap], "dma")

    def mm_group(self, fns_ins, out_ap):
        self._gid += 1
        gid = self._gid
        last = None
        for fn, ins in fns_ins:
            last = self._record("tensor", fn, list(ins), [out_ap], "op", gid=gid)
        return last

    def truncate(self, n):
        keep = self.all_ops[:n]
        ks = set(map(id, keep))
        self.all_ops = keep
        for e in self.ENGINES:
            self.ops[e] = [o for o in self.ops[e] if id(o) in ks]
        for tn in list(self.hist):
            self.hist[tn] = [(k, w, o) for (k, w, o) in self.hist[tn]
                             if id(o) in ks]

    def compact(self, name):
        self.hist.pop(name, None)

    def build(self):
        nc = self.nc
        esem = {e: self.es.enter_context(nc.semaphore("S_" + e)) for e in self.ENGINES}
        dsem = {e: self.es.enter_context(nc.semaphore("D_" + e)) for e in self.ENGINES}

        def is_same_engine_edge(cons, prod):
            if prod.eng != cons.eng:
                return False
            if prod.kind == "dma" or cons.kind == "dma":
                return False
            if prod.eng == "tensor" and not TE_DRAIN:
                return True
            return True

        for op in self.all_ops:
            prods = self.dep_map.get(id(op))
            if not prods:
                continue
            best = {}
            for p in prods:
                key = (p.eng, p.kind == "dma")
                if key not in best or p.seq > best[key].seq:
                    best[key] = p
            self.dep_map[id(op)] = set(best.values())
        for op in self.all_ops:
            for p in self.dep_map.get(id(op), ()):
                if not is_same_engine_edge(op, p):
                    p.needed = True
        for e in self.ENGINES:
            c_eng = 0
            c_dma = 0
            for o in self.ops[e]:
                if o.kind == "dma":
                    c_dma += 16
                    o.inc_val = ("D", e, c_dma)
                    o.needed = True
                elif o.needed:
                    c_eng += 1
                    o.inc_val = ("S", e, c_eng)
        bank_state = {}
        n_tile_drains = 0
        for o in self.ops["tensor"]:
            if o.kind != "op":
                continue
            outs_ps = [k for k in o.outs if k[0] in self.psum_names]
            if not outs_ps:
                continue
            rowlo = min((k[1] for k in o.ins), default=0)
            rowhi = max((k[2] for k in o.ins), default=128)
            if rowhi - rowlo > 64:
                bank_state.clear()
                continue
            conflict = False
            for k in outs_ps:
                for b in range(k[3] // 2048, (k[4] - 1) // 2048 + 1):
                    for (rl, rh, pl, ph) in bank_state.get((k[0], b), ()):
                        if (pl < k[2] and k[1] < ph
                                and (rl, rh) != (rowlo, rowhi)):
                            conflict = True
                            break
            if conflict:
                o.drain_before = True
                n_tile_drains += 1
                bank_state.clear()
            for k in outs_ps:
                for b in range(k[3] // 2048, (k[4] - 1) // 2048 + 1):
                    ent = (rowlo, rowhi, k[1], k[2])
                    lst = bank_state.setdefault((k[0], b), [])
                    if ent not in lst:
                        lst.append(ent)
        self.n_tile_drains = n_tile_drains

        order = {}
        for i, o in enumerate(self.all_ops):
            order[id(o)] = i
        dma_prefix = {}
        run = {e: 0 for e in self.ENGINES}
        dma_wait_val = {}
        for o in self.all_ops:
            prods = self.dep_map.get(id(o), ())
            if any(p.kind == "dma" for p in prods):
                dma_wait_val[id(o)] = dict(run)
            if o.kind == "dma":
                run[o.eng] += 16
        watermark = {e: {} for e in self.ENGINES}
        last_drained_seq = {e: -1 for e in self.ENGINES}
        for e in self.ENGINES:
            for o in self.ops[e]:
                if o.drain_before:
                    last_drained_seq[e] = o.seq - 1
                need_drain = -1
                for p in self.dep_map.get(id(o), ()):
                    if is_same_engine_edge(o, p):
                        if p.eng == "tensor" and not TE_DRAIN:
                            continue
                        need_drain = max(need_drain, p.seq)
                    else:
                        kind, peng, val = p.inc_val
                        if kind == "D":
                            val = dma_wait_val[id(o)].get(peng, val)
                        wm = watermark[e].get((kind, peng), 0)
                        if val > wm:
                            o.waits.append((kind, peng, val))
                            watermark[e][(kind, peng)] = val
                if need_drain > last_drained_seq[e]:
                    o.drain_before = True
                    last_drained_seq[e] = o.seq - 1

        waited_dma = {e: set() for e in self.ENGINES}
        for e in self.ENGINES:
            for o in self.ops[e]:
                for (kind, peng, val) in o.waits:
                    if kind == "D":
                        waited_dma[peng].add(val)
        fence_map = {}
        for e in self.ENGINES:
            wvals = sorted(waited_dma[e])
            cum = 0
            wi = 0
            fenced_to = 0
            for o in self.ops[e]:
                if o.kind != "dma":
                    continue
                nxt = cum + 16
                while wi < len(wvals) and wvals[wi] <= cum:
                    if wvals[wi] > fenced_to:
                        fence_map[id(o)] = wvals[wi]
                        fenced_to = wvals[wi]
                    wi += 1
                cum = nxt
        block = self.es.enter_context(nc.Block())
        for e in self.ENGINES:
            ops = self.ops[e]
            dsm = dsem[e]

            def body(engine, ops=ops, dsm=dsm):
                for o in ops:
                    if o.drain_before:
                        engine.drain()
                    fv = fence_map.get(id(o))
                    if fv is not None:
                        engine.wait_ge(dsm, fv)
                    for (kind, peng, val) in o.waits:
                        sm = esem[peng] if kind == "S" else dsem[peng]
                        engine.wait_ge(sm, val)
                    instr = o.fn(engine)
                    if o.inc_val is not None:
                        kind, peng, val = o.inc_val
                        sm = esem[peng] if kind == "S" else dsem[peng]
                        instr.then_inc(sm, 16 if o.kind == "dma" else 1)
            getattr(block, e)(body)


def build_program():
    nc = bass.Bass("TRN2", target_bir_lowering=False)
    s = Prog(nc)

    def din(name, shape, dtype=F32):
        return nc.dram_tensor(name, list(shape), dtype, kind="ExternalInput").ap()

    X = din("x", [T, D])
    Wqkv = din("aa_wqkv", [D, 3 * D])
    BqkCol = din("aa_bqk_col", [128, 16])
    BvQ = din("aa_bvq", [1, D], F32)
    BgateQ = din("aa_bgateq", [1, D], F32)
    Wout = din("aa_wout", [D, D])
    BoutQ = din("aa_boutq", [1, D], F32)
    Wgate = din("aa_wgate", [D, D])
    PosBQ = din("aa_posbq", [128, 256])
    SE2 = din("aa_se2", [128, NOFF])
    IfG = din("aa_ifg", [1, HA])
    MaskA = din("aa_mask", [128, 13 * NOFF])

    FWq = din("f_wq", [D, D])
    FWdown = din("f_wdown", [D, FL]); FBdownCol = din("f_bdown_col", [64, 1])
    FWupK = din("f_wupk", [64, 1024])
    FWupV = din("f_wupv", [64, 1024])
    FWo = din("f_wo", [D, D]); FBoRow = din("f_bo", [1, D])
    FG1Col = din("f_g1_col", [128, 8])
    FG2Col = din("f_g2_col", [128, 8])
    FWg = din("f_wg", [D, FE]); FBgRow = din("f_bg", [1, FE])
    FWe = din("f_we", [FE * D, D])
    FBoQ = din("f_boq", [1, D])
    FBupvQ = din("f_bupvq", [1, D])
    FBqColH = din("f_bq_colh", [64, 16])
    FBupkColH = din("f_bupk_colh", [64, 16])

    MWin = din("m_win", [D, M_PROJ])
    MConvWBc = din("m_convw_bc", [128, 4 * M_CONVD])
    MConvBRow = din("m_convb_row", [1, M_CONVD])
    MDtB = din("m_dtb", [1, MH])
    MAlogBc = din("m_alog_bc", [128, MH])
    MDBc = din("m_d_bc", [128, MH])
    MNwBc = din("m_nw_bc", [128, M_INTER])
    MWout = din("m_wout", [M_INTER, D])

    CIdent = din("c_ident", [128, 128], BF16)
    CSeg64 = din("c_seg64", [128, 2], BF16)
    CTriuF = din("c_triuf", [128, 128], F32)
    CIdentF = din("c_identf", [128, 128], F32)
    CMaskW = din("c_maskw", [128, 16 * MCH], BF16)

    OUT = nc.dram_tensor("out", [T, D], F32, kind="ExternalOutput").ap()
    VSCR = nc.dram_tensor("v_scratch", [T, D], BF16, kind="Internal").ap()
    QSCR = nc.dram_tensor("q_scratch", [NF * 128, T], BF16, kind="Internal").ap()
    FSCR = nc.dram_tensor("f_scratch", [T, D], BF16, kind="Internal").ap()
    W4SCR = nc.dram_tensor("w4_scratch", [D, 4 * M_CONVD], BF16, kind="Internal").ap()
    HSCR = nc.dram_tensor("h_scratch", [T, M_INTER], BF16, kind="Internal").ap()
    GSCR = nc.dram_tensor("g_scratch", [T, M_INTER], BF16, kind="Internal").ap()
    BCSCR = nc.dram_tensor("bc_scratch", [T, 2 * MN], BF16, kind="Internal").ap()
    YSCR = nc.dram_tensor("y_scratch", [T, M_INTER], BF16, kind="Internal").ap()
    dbg_outs = {nm: nc.dram_tensor(nm, list(shp), dt, kind="ExternalOutput").ap()
                for (nm, shp, dt) in DBG}

    xt = s.sbuf("xt", [128, NT, D], F32)
    qfm = s.sbuf("qfm", [128, NF, 512], BF16)
    kfm = s.sbuf("kfm", [128, NF, T], BF16)
    gtm = s.sbuf("gtm", [128, NT // 2, D], BF16)
    arenaA = s.sbuf("arenaA", [128, 16 * 1024], BF16)
    arenaW = s.sbuf("arenaW", [128, 8 * 1024], BF16)
    qkbuf = s.sbuf("qkbuf", [128, 4 * NOFF * HA], F32)
    small = s.sbuf("small", [128, 768], F32)
    bcol = s.sbuf("bcol", [128, 64], F32)
    psc = s.sbuf("psc", [128, NOFF * HA], BF16)
    vsh = s.sbuf("vsh", [128, 2, D], BF16)
    aofm = s.sbuf("aofm", [128, NF, 128], BF16)
    consts = s.sbuf("consts", [128, 8], F32)
    cident = s.sbuf("cident", [128, 128], BF16)
    cseg64 = s.sbuf("cseg64", [128, 2], BF16)
    prow = s.sbuf("prow", [128, 720], F32)
    pbq = s.sbuf("pbq", [128, 256], F32)
    ifgr = s.sbuf("ifgr", [1, 16], F32)
    bq3 = s.sbuf("bq3", [128, 3 * 352], BF16)
    onesf = s.sbuf("onesf", [128, 128], F32)
    onesb = s.sbuf("onesb", [128, 128], BF16)
    amask = s.sbuf("amask", [128, 13 * NOFF], BF16)
    se2 = s.sbuf("se2", [128, NOFF], BF16)

    bsc_pv = s.sbuf("bsc_pv", [128, 3 * D], BF16)
    psA0 = s.psum([128, 512], F32)
    psA1 = s.psum([128, 512], F32)
    psB = s.psum([128, 2048], F32)
    psT0 = s.psum([128, 1024], BF16)
    psT1 = s.psum([128, 1024], BF16)

    s.dma("sync", xt[:], X.rearrange("(n p) d -> p n d", p=128))
    s.dma("sync", cident[:], CIdent)
    s.dma("sync", cseg64[:], CSeg64)
    s.dma("sync", bcol[:, 0:16], BqkCol)
    s.dma("gpsimd", amask[:], MaskA)
    s.dma("gpsimd", se2[:], SE2)
    s.dma("sync", pbq[:], PosBQ)
    s.dma("sync", ifgr[:], IfG)
    s.dma("gpsimd", bq3[0:1, 0:1024], BvQ)
    s.dma("gpsimd", bq3[32:33, 0:1024], BgateQ)
    s.dma("gpsimd", bq3[64:65, 0:1024], BoutQ)
    s.op("vector", lambda e: e.memset(onesf[:], 1.0), outs=[onesf[:]])
    s.op("vector", lambda e: e.memset(onesb[:], 1.0), outs=[onesb[:]])
    s.op("vector", lambda e: e.memset(consts[:, 0:1], 1e-5), outs=[consts[:, 0:1]])
    s.op("vector", lambda e: e.memset(consts[:, 1:2], 1e-6), outs=[consts[:, 1:2]])


    for qq, (c0, w) in enumerate(((0, 256), (256, 256), (512, 192))):
        rhsq = pbq[32 * qq:32 * qq + 1, 0:w]
        dstq = psB[:, 512 * qq:512 * qq + w]
        o1 = onesf[32 * qq:32 * qq + 1, :]
        tp = (32 * qq, 0) if qq else None
        s.mm_group([(lambda e, rhsq=rhsq, dstq=dstq, o1=o1, tp=tp: e.matmul(
            out=dstq, lhsT=o1, rhs=rhsq, start=True, stop=True,
            tile_position=tp), [o1, rhsq])], dstq)
        s.op("vector", lambda e, qq=qq, c0=c0, w=w, dstq=dstq: e.tensor_copy(
            out=prow[:, c0:c0 + w], in_=dstq),
            ins=[dstq], outs=[prow[:, c0:c0 + w]])
    s.mm_group([(lambda e: e.matmul(out=psB[:, 1536:1552], lhsT=onesf[0:1, :],
                                    rhs=ifgr[:], start=True, stop=True),
                 [onesf[0:1, :], ifgr[:]])], psB[:, 1536:1552])
    s.op("vector", lambda e: e.tensor_copy(out=prow[:, 704:720], in_=psB[:, 1536:1552]),
         ins=[psB[:, 1536:1552]], outs=[prow[:, 704:720]])

    def norm_tile(src_ap, dst_ap, eps_col, subtract_mean, junk_ap=None):
        W = src_ap.shape[-1]
        junk = junk_ap if junk_ap is not None else dst_ap
        c = lambda j: small[:, 736 + j:736 + j + 1]
        s.op("scalar", lambda e: e.activation(out=junk, in_=src_ap,
                                              func=AF.Square, accum_out=c(0)),
             ins=[src_ap], outs=[junk, c(0)])
        if subtract_mean:
            s.op("vector", lambda e: e.tensor_reduce(out=c(1), in_=src_ap,
                                                     axis=AX.X, op=ALU.add),
                 ins=[src_ap], outs=[c(1)])
            s.op("vector", lambda e: e.tensor_scalar(out=c(2), in0=c(1),
                                                     scalar1=1.0 / W, scalar2=None,
                                                     op0=ALU.mult),
                 ins=[c(1)], outs=[c(2)])
            s.op("vector", lambda e: e.tensor_tensor(out=c(3), in0=c(2), in1=c(2),
                                                     op=ALU.mult),
                 ins=[c(2)], outs=[c(3)])
            s.op("vector", lambda e: e.tensor_scalar(out=c(4), in0=c(0),
                                                     scalar1=1.0 / W, scalar2=None,
                                                     op0=ALU.mult),
                 ins=[c(0)], outs=[c(4)])
            s.op("vector", lambda e: e.tensor_tensor(out=c(5), in0=c(4), in1=c(3),
                                                     op=ALU.subtract),
                 ins=[c(4), c(3)], outs=[c(5)])
        else:
            s.op("vector", lambda e: e.tensor_scalar(out=c(5), in0=c(0),
                                                     scalar1=1.0 / W, scalar2=None,
                                                     op0=ALU.mult),
                 ins=[c(0)], outs=[c(5)])
        s.op("scalar", lambda e: e.activation(out=c(6), in_=c(5), func=AF.Sqrt,
                                              bias=consts[:, eps_col:eps_col + 1]),
             ins=[c(5), consts[:, eps_col:eps_col + 1]], outs=[c(6)])
        s.op("vector", lambda e: e.reciprocal(out=c(7), in_=c(6)),
             ins=[c(6)], outs=[c(7)])
        if subtract_mean:
            s.op("vector", lambda e: e.tensor_scalar(out=dst_ap, in0=src_ap,
                                                     scalar1=c(2), scalar2=c(7),
                                                     op0=ALU.subtract, op1=ALU.mult),
                 ins=[src_ap, c(2), c(7)], outs=[dst_ap])
        else:
            s.op("vector", lambda e: e.tensor_scalar(out=dst_ap, in0=src_ap,
                                                     scalar1=c(7), scalar2=None,
                                                     op0=ALU.mult),
                 ins=[src_ap, c(7)], outs=[dst_ap])

    def bias_piece_mms(mm, ps, vg, bias_block, start_flag=False):
        pb = 32 * bias_block
        rhsb = bq3[pb:pb + 1, 512 * vg:512 * vg + 512]
        o1 = onesb[pb:pb + 1, :]
        tp = (pb, 0) if pb else None
        mm.append((lambda e, rhsb=rhsb, ps=ps, o1=o1, tp=tp: e.matmul(
            out=ps[:], lhsT=o1, rhs=rhsb,
            start=False, stop=True, tile_position=tp), [o1, rhsb]))

    tpslot = [0]

    def transpose128(src_ap, dst_ap):
        slot = tpslot[0] % 2
        tpslot[0] += 1
        p = (psT0 if slot == 0 else psT1)[:, 0:128]
        s.op("tensor", lambda e, p=p: e.transpose(out=p, in_=src_ap, identity=cident[:]),
             ins=[src_ap, cident[:]], outs=[p])
        s.op("vector", lambda e, p=p: e.tensor_copy(out=dst_ap, in_=p),
             ins=[p], outs=[dst_ap])

    ln_tm = arenaA[:, 0:8 * D].rearrange("p (n d) -> p n d", d=D)
    ln_fm = arenaA[:, 8 * D:16 * D].rearrange("p (c t) -> p c t", t=1024)
    wA = arenaW[:, 0:8 * 1024].rearrange("p (n c) -> p n c", c=1024)
    ESL = 512
    e_slots = arenaA[:, 0:3 * 8 * ESL].rearrange("p (sl c w) -> p sl c w", sl=3, w=ESL)
    VS = VSCR.rearrange("(n p) d -> p n d", p=128)

    def attn_a():
        for half in range(2):
            t0 = half * (T // 2)
            tiles = list(range(half * 8, half * 8 + 8))

            for i in tiles:
                li = i - half * 8
                norm_tile(xt[:, i, :], ln_tm[:, li, :], 0, True)
                for cc in range(NF):
                    transpose128(ln_tm[:, li, 128 * cc:128 * (cc + 1)],
                                 ln_fm[:, cc, 128 * li:128 * li + 128])

            def load_w_1024(W_dram, col_off):
                src = W_dram.rearrange("(n p) c -> p n c", p=128)[:, :, col_off:col_off + 1024]
                s.dma("gpsimd", wA[:, :, :], src)

            def proj_fm(dst, bias_idx, t_off):
                for oc in range(NF):
                    for tg in range(2):
                        ps = psA0 if (oc * 2 + tg) % 2 == 0 else psA1
                        mm = [(lambda e, oc=oc, tg=tg, fc=fc, ps=ps: e.matmul(
                            out=ps[:], lhsT=wA[:, fc, 128 * oc:128 * (oc + 1)],
                            rhs=ln_fm[:, fc, 512 * tg:512 * (tg + 1)],
                            start=(fc == 0), stop=(fc == NF - 1)),
                            [wA[:, fc, 128 * oc:128 * (oc + 1)],
                             ln_fm[:, fc, 512 * tg:512 * (tg + 1)]]) for fc in range(NF)]
                        s.mm_group(mm, ps[:])
                        dstap = dst(oc, tg)
                        s.op("vector", lambda e, dstap=dstap, ps=ps, oc=oc: e.tensor_scalar(
                            out=dstap, in0=ps[:],
                            scalar1=bcol[:, bias_idx + oc:bias_idx + oc + 1],
                            scalar2=None, op0=ALU.add),
                            ins=[ps[:], bcol[:, bias_idx + oc:bias_idx + oc + 1]],
                            outs=[dstap])

            load_w_1024(Wqkv, 0)
            for tg in range(2):
                for oc in range(NF):
                    ps = psA0 if oc % 2 == 0 else psA1
                    mm = [(lambda e, oc=oc, tg=tg, fc=fc, ps=ps: e.matmul(
                        out=ps[:], lhsT=wA[:, fc, 128 * oc:128 * (oc + 1)],
                        rhs=ln_fm[:, fc, 512 * tg:512 * (tg + 1)],
                        start=(fc == 0), stop=(fc == NF - 1)),
                        [wA[:, fc, 128 * oc:128 * (oc + 1)],
                         ln_fm[:, fc, 512 * tg:512 * (tg + 1)]]) for fc in range(NF)]
                    s.mm_group(mm, ps[:])
                    s.op("vector", lambda e, oc=oc, ps=ps: e.tensor_scalar(
                        out=qfm[:, oc, :], in0=ps[:],
                        scalar1=bcol[:, oc:oc + 1], scalar2=None, op0=ALU.add),
                        ins=[ps[:], bcol[:, oc:oc + 1]], outs=[qfm[:, oc, :]])
                qq0 = 512 * (2 * half + tg)
                s.dma("sync", QSCR.rearrange("(c p) t -> p c t", p=128)[:, :, qq0:qq0 + 512],
                      qfm[:])
            load_w_1024(Wqkv, D)
            proj_fm(lambda oc, tg: kfm[:, oc, t0 + 512 * tg:t0 + 512 * (tg + 1)], 8, t0)

            load_w_1024(Wqkv, 2 * D)
            for i in tiles:
                li = i - half * 8
                slot = li % 2
                for vg in range(2):
                    ps = psA0 if vg == 0 else psA1
                    mm = [(lambda e, li=li, vg=vg, fc=fc, ps=ps: e.matmul(
                        out=ps[:], lhsT=ln_fm[:, fc, 128 * li:128 * (li + 1)],
                        rhs=wA[:, fc, 512 * vg:512 * (vg + 1)],
                        start=(fc == 0), stop=False),
                        [ln_fm[:, fc, 128 * li:128 * (li + 1)],
                         wA[:, fc, 512 * vg:512 * (vg + 1)]]) for fc in range(NF)]
                    bias_piece_mms(mm, ps, vg, 0)
                    s.mm_group(mm, ps[:])
                    s.op("vector", lambda e, slot=slot, vg=vg, ps=ps: e.tensor_copy(
                        out=vsh[:, slot, 512 * vg:512 * (vg + 1)], in_=ps[:]),
                        ins=[ps[:]], outs=[vsh[:, slot, 512 * vg:512 * (vg + 1)]])
                s.dma("sync", VS[:, i, :], vsh[:, slot, :])

            load_w_1024(Wgate, 0)
            for i in tiles:
                li = i - half * 8
                for vg in range(2):
                    ps = psA0 if vg == 0 else psA1
                    mm = [(lambda e, li=li, vg=vg, fc=fc, ps=ps: e.matmul(
                        out=ps[:], lhsT=ln_fm[:, fc, 128 * li:128 * (li + 1)],
                        rhs=wA[:, fc, 512 * vg:512 * (vg + 1)],
                        start=(fc == 0), stop=False),
                        [ln_fm[:, fc, 128 * li:128 * (li + 1)],
                         wA[:, fc, 512 * vg:512 * (vg + 1)]]) for fc in range(NF)]
                    bias_piece_mms(mm, ps, vg, 1)
                    s.mm_group(mm, ps[:])
                    s.op("scalar", lambda e, li=li, vg=vg, ps=ps: e.activation(
                        out=gtm[:, li, 512 * vg:512 * (vg + 1)], in_=ps[:],
                        func=AF.Sigmoid),
                        ins=[ps[:]], outs=[gtm[:, li, 512 * vg:512 * (vg + 1)]])

            load_w_1024(Wout, 0)

            for qq in range(2):
                Q = half * 2 + qq
                qt0 = 512 * Q
                s.dma("sync", qfm[:],
                      QSCR.rearrange("(c p) t -> p c t", p=128)[:, :, qt0:qt0 + 512])
                s.op("vector", lambda e: e.memset(qkbuf[:], 0.0), outs=[qkbuf[:]])
                esli = [0]
                for og in range(2):
                    o_list = list(enumerate(OFFSETS))[og * 22:(og + 1) * 22]
                    for gi, (o_i, o) in enumerate(o_list):
                        if qt0 + 512 <= o:
                            continue
                        a = max(0, o - qt0)
                        for fc in range(NF):
                            sl = esli[0] % 3
                            esli[0] += 1
                            slot = e_slots[:, sl, fc, :]
                            if a > 0:
                                s.op("vector", lambda e, slot=slot, a=a: e.memset(
                                    slot[:, 0:a], 0.0), outs=[slot[:, 0:a]])
                            qs_ap = qfm[:, fc, a:512]
                            ks_ap = kfm[:, fc, qt0 + a - o:qt0 + 512 - o]
                            s.op("vector", lambda e, slot=slot, a=a, qs_ap=qs_ap,
                                 ks_ap=ks_ap: e.tensor_tensor(
                                     out=slot[:, a:512], in0=qs_ap, in1=ks_ap,
                                     op=ALU.mult),
                                 ins=[qs_ap, ks_ap], outs=[slot[:, a:512]])
                            for li in range(4):
                                ti = 4 * Q + li
                                if 128 * (ti + 1) <= o:
                                    continue
                                lhsT = slot[:, 128 * li:128 * (li + 1)]
                                col = li * 352 + gi * 16 + fc * 2
                                dst = psB[:, col:col + 2]
                                s.mm_group([(lambda e, lhsT=lhsT, dst=dst: e.matmul(
                                    out=dst, lhsT=lhsT, rhs=cseg64[:],
                                    start=True, stop=True), [lhsT, cseg64[:]])], dst)
                    for li in range(4):
                        src = psB[:, li * 352:(li + 1) * 352].rearrange(
                            "p (i c) -> p c i", c=16)
                        dst = qkbuf[:, li * NOFF * HA:(li + 1) * NOFF * HA].rearrange(
                            "p (h i) -> p h i", i=NOFF)[:, :, og * 22:(og + 1) * 22]
                        s.op("vector", lambda e, src=src, dst=dst: e.tensor_copy(
                            out=dst, in_=src),
                            ins=[psB[:, li * 352:(li + 1) * 352]],
                            outs=[qkbuf[:, li * NOFF * HA:(li + 1) * NOFF * HA]])

                for li in range(4):
                    ti = 4 * Q + li
                    tt0 = 128 * ti
                    qk_t = qkbuf[:, li * NOFF * HA:(li + 1) * NOFF * HA]
                    for h in range(HA):
                        fc = h // 2
                        base = (h % 2) * 64
                        qh = qfm[:, fc, tt0 - qt0:tt0 - qt0 + 128][base:base + 64, :]
                        seh = se2[base:base + 64, :]
                        qb0 = 1024 + 512 * (h % 2)
                        dstp = psB[:, qb0 + (h // 2) * NOFF:qb0 + (h // 2 + 1) * NOFF]
                        tp = (base, 0) if base else None
                        s.mm_group([(lambda e, qh=qh, seh=seh, dstp=dstp, tp=tp: e.matmul(
                            out=dstp, lhsT=qh, rhs=seh, start=True, stop=True,
                            tile_position=tp), [qh, seh])], dstp)
                    smv = small[:, 0:704].rearrange("p (h i) -> p h i", i=NOFF)
                    for hh in range(2):
                        pq = psB[:, 1024 + 512 * hh:1024 + 512 * hh + 352]
                        dsv = smv[:, hh:HA:2, :]
                        psv8 = pq.rearrange("p (h i) -> p h i", i=NOFF)
                        s.op("vector", lambda e, psv8=psv8, dsv=dsv: e.tensor_scalar(
                            out=dsv, in0=psv8, scalar1=INV_A, scalar2=INV_A,
                            op0=ALU.mult, op1=ALU.add),
                            ins=[pq],
                            outs=[small[:, 0:704]])
                    s.op("vector", lambda e, qk_t=qk_t: e.tensor_tensor(
                        out=small[:, 0:704], in0=small[:, 0:704], in1=qk_t, op=ALU.mult),
                        ins=[small[:, 0:704], qk_t], outs=[small[:, 0:704]])
                    s.op("vector", lambda e: e.tensor_tensor(
                        out=small[:, 0:704], in0=small[:, 0:704], in1=prow[:, 0:704],
                        op=ALU.add),
                        ins=[small[:, 0:704], prow[:, 0:704]], outs=[small[:, 0:704]])
                    if ti < 13:
                        mt = amask[:, ti * NOFF:(ti + 1) * NOFF]
                        mexp = mt.unsqueeze(1).broadcast_to([128, HA, NOFF])
                        lgv = small[:, 0:704].rearrange("p (h i) -> p h i", i=NOFF)
                        s.op("vector", lambda e, lgv=lgv, mexp=mexp: e.tensor_tensor(
                            out=lgv, in0=lgv, in1=mexp, op=ALU.add),
                            ins=[small[:, 0:704], mt], outs=[small[:, 0:704]])
                    s.op("scalar", lambda e: e.activation(out=psc[:], in_=small[:, 0:704],
                                                          func=AF.Exp),
                         ins=[small[:, 0:704]], outs=[psc[:]])
                    s.op("vector", lambda e: e.tensor_reduce(
                        out=small[:, 704:704 + HA],
                        in_=psc[:].rearrange("p (h i) -> p h i", i=NOFF),
                        axis=AX.X, op=ALU.add),
                        ins=[psc[:]], outs=[small[:, 704:704 + HA]])
                    s.op("vector", lambda e: e.reciprocal(out=small[:, 720:720 + HA],
                                                          in_=small[:, 704:704 + HA]),
                         ins=[small[:, 704:704 + HA]], outs=[small[:, 720:720 + HA]])
                    s.op("vector", lambda e: e.tensor_tensor(
                        out=small[:, 720:720 + HA], in0=small[:, 720:720 + HA],
                        in1=prow[:, 704:704 + HA], op=ALU.mult),
                        ins=[small[:, 720:720 + HA], prow[:, 704:704 + HA]],
                        outs=[small[:, 720:720 + HA]])

                    pacc = psB[:, 0:1024]
                    valid_offs = [(o_i, o) for (o_i, o) in enumerate(OFFSETS)
                                  if 128 * (ti + 1) > o]
                    nvo = len(valid_offs)
                    for vi, (o_i, o) in enumerate(valid_offs):
                        vslot = vsh[:, vi % 2, :]
                        r0 = max(0, o - tt0)
                        src_lo = tt0 + r0 - o
                        if r0 > 0:
                            s.op("vector", lambda e, vslot=vslot, r0=r0: e.memset(
                                vslot[0:r0, :], 0.0), outs=[vslot[0:r0, :]])
                        s.dma("sync", vslot[r0:128, :],
                              VSCR[src_lo:src_lo + (128 - r0), :])
                        pexp = psc[:].rearrange("p (h i) -> p h i", i=NOFF)[
                            :, :, o_i:o_i + 1].broadcast_to([128, HA, 64])
                        tmp = bsc_pv[:, (vi % 3) * D:((vi % 3) + 1) * D]
                        s.op("vector", lambda e, vslot=vslot, pexp=pexp, tmp=tmp:
                             e.tensor_tensor(
                                 out=tmp.rearrange("p (h d) -> p h d", d=64),
                                 in0=vslot.rearrange("p (h d) -> p h d", d=64),
                                 in1=pexp, op=ALU.mult),
                             ins=[vslot[:], psc[:]], outs=[tmp[:]])
                        last = (vi == nvo - 1)
                        for g2 in range(2):
                            fn = (lambda e, tmp=tmp, g2=g2, first=(vi == 0), last=last:
                                  e.matmul(out=pacc[:, 512 * g2:512 * (g2 + 1)],
                                           lhsT=cident[:],
                                           rhs=tmp[:, 512 * g2:512 * (g2 + 1)],
                                           start=first, stop=last))
                            ins_ = [cident[:], tmp[:, 512 * g2:512 * (g2 + 1)]]
                            outs_ = [pacc] if (last and g2 == 1) else []
                            s._record("tensor", fn, ins_, outs_, "op")

                    zexp = small[:, 720:720 + HA].unsqueeze(2).broadcast_to([128, HA, 64])
                    aot = bsc_pv[:, ((nvo + 1) % 3) * D:((nvo + 1) % 3 + 1) * D]
                    s.op("vector", lambda e, zexp=zexp, aot=aot: e.tensor_tensor(
                        out=aot.rearrange("p (h d) -> p h d", d=64), in0=pacc.rearrange(
                            "p (h d) -> p h d", d=64), in1=zexp, op=ALU.mult),
                        ins=[pacc, small[:, 720:720 + HA]], outs=[aot[:]])
                    gti = gtm[:, ti - half * 8, :]
                    s.op("vector", lambda e, aot=aot, gti=gti: e.tensor_tensor(
                        out=aot, in0=aot, in1=gti, op=ALU.mult),
                        ins=[aot[:], gti], outs=[aot[:]])
                    for cc in range(NF):
                        transpose128(aot[:, 128 * cc:128 * (cc + 1)], aofm[:, cc, :])
                    for vg in range(2):
                        ps = psA0 if vg == 0 else psA1
                        mm = [(lambda e, fc=fc, vg=vg, ps=ps: e.matmul(
                            out=ps[:], lhsT=aofm[:, fc, :],
                            rhs=wA[:, fc, 512 * vg:512 * (vg + 1)],
                            start=(fc == 0), stop=False),
                            [aofm[:, fc, :], wA[:, fc, 512 * vg:512 * (vg + 1)]])
                            for fc in range(NF)]
                        bias_piece_mms(mm, ps, vg, 2)
                        s.mm_group(mm, ps[:])
                        xslice = xt[:, ti, 512 * vg:512 * (vg + 1)]
                        s.op("vector", lambda e, xslice=xslice, ps=ps: e.tensor_tensor(
                            out=xslice, in0=xslice, in1=ps[:], op=ALU.add),
                            ins=[xslice, ps[:]], outs=[xslice])

    f_u = arenaA[:, 0:2048].rearrange("p (n d) -> p n d", d=D)
    f_hfm = arenaA[:, 2048:4096].rearrange("p (c t) -> p c t", t=256)
    f_vin = arenaA[:, 4096:8192].bitcast(F32).rearrange("p (n d) -> p n d", d=D)
    f_wdown = arenaA[:, 8192:8704].rearrange("p (c w) -> p c w", w=64)
    f_wupk = arenaA[:, 8704:9728][0:64, :]
    f_wupv = arenaA[:, 9728:10752][0:64, :]
    f_wupvb = arenaA[0:65, 9728:10752]
    f_wg = arenaA[:, 10752:10816].rearrange("p (c w) -> p c w", w=8)
    f_bg = arenaA[:, 10816:10824]
    f_bo = arenaA[0:1, 11976:13000]
    f_zero = arenaA[:, 11584:11712][0:64, :]
    f_bcolh = arenaA[:, 11712:11840].bitcast(F32)[0:64, 0:16]
    f_bcolh2 = arenaA[:, 11840:11968].bitcast(F32)[0:64, 0:16]
    f_bdcol = arenaA[:, 11968:11976].bitcast(F32)[0:64, 0:1]
    f_qpt = gtm[:].rearrange("p n d -> p (n d)")[0:64, 0:4096].rearrange(
        "p (h t) -> p h t", t=256)
    f_kpt = gtm[:].rearrange("p n d -> p (n d)")[0:64, 4096:8192].rearrange(
        "p (h t) -> p h t", t=256)
    f_z = qfm[:].rearrange("p c t -> p (c t)")[0:64, 0:256]
    f_zz = qfm[:].rearrange("p c t -> p (c t)")[0:65, 0:256]
    f_esb = qfm[:].rearrange("p c t -> p (c t)")[:, 256:768]
    f_vtp = qfm[:].rearrange("p c t -> p (c t)")[:, 768:768 + 32 * 65]
    f_pvn = bsc_pv[:, 0:2048]
    f_hrt = bsc_pv[:, 2048:3072]
    f_aofm = qkbuf[:].bitcast(BF16)[:, 0:2048].rearrange(
        "p (c t) -> p c t", t=256)
    f_gate = qkbuf[:].bitcast(BF16)[:, 2048:2048 + 256].bitcast(F32)
    f_h2fm = kfm[:]
    FV = FSCR.rearrange("(n p) d -> p n d", p=128)

    def ffn_load_shared():
        s.dma("gpsimd", f_wdown, FWdown.rearrange("(c p) w -> p c w", p=128))
        s.dma("gpsimd", f_wupk, FWupK)
        s.dma("gpsimd", f_wupv, FWupV)
        s.dma("gpsimd", f_wg, FWg.rearrange("(c p) w -> p c w", p=128))
        s.dma("gpsimd", f_bg[0:1, :], FBgRow)
        s.dma("gpsimd", f_bo, FBoQ)
        s.dma("gpsimd", arenaA[64:65, 9728:10752], FBupvQ)
        s.op("vector", lambda e: e.memset(
            qfm[:].rearrange("p c t -> p (c t)")[64:65, 0:256], 1.0),
            outs=[qfm[:].rearrange("p c t -> p (c t)")[64:65, 0:256]])
        s.dma("sync", f_bcolh, FBqColH)
        s.dma("sync", f_bcolh2, FBupkColH)
        s.dma("sync", f_bdcol, FBdownCol)
        s.dma("sync", bcol[:, 32:40], FG1Col)
        s.dma("sync", bcol[:, 40:48], FG2Col)
        s.op("vector", lambda e: e.memset(f_zero, 0.0), outs=[f_zero])
        s.op("vector", lambda e: e.memset(
            f_vtp.rearrange("p (g w) -> p g w", w=65)[:, :, 64:65], 1.0),
            outs=[f_vtp])

    def f_bias_pieces(mm, ps, vg, row_ap, pb):
        rhsb = row_ap[:, 512 * vg:512 * vg + 512]
        o1 = onesb[pb:pb + 1, :]
        tp = (pb, 0) if pb else None
        mm.append((lambda e, rhsb=rhsb, o1=o1, ps=ps, tp=tp: e.matmul(
            out=ps[:], lhsT=o1, rhs=rhsb,
            start=False, stop=True, tile_position=tp), [o1, rhsb]))

    def f_transpose_scale(src, dst, gcol):
        slot = tpslot[0] % 2
        tpslot[0] += 1
        p = (psT0 if slot == 0 else psT1)[:, 0:128]
        s.op("tensor", lambda e, p=p, src=src: e.transpose(
            out=p, in_=src, identity=cident[:]), ins=[src, cident[:]], outs=[p])
        s.op("vector", lambda e, p=p, dst=dst, gcol=gcol: e.tensor_scalar(
            out=dst, in0=p, scalar1=gcol, scalar2=None, op0=ALU.mult),
            ins=[p, gcol], outs=[dst])

    def ffn(call):
        ffn_load_shared()
        for psx in (psA0, psA1):
            s.op("vector", lambda e, psx=psx: e.memset(psx[:], 0.0), outs=[psx[:]])
        s.op("vector", lambda e: e.memset(psB[:], 1.0), outs=[psB[:]])
        for ch in range(8):
            ct0 = 256 * ch
            tiles = [2 * ch, 2 * ch + 1]
            for j, i in enumerate(tiles):
                norm_tile(xt[:, i, :], f_u[:, j, :], 0, True)
                norm_tile(f_u[:, j, :], f_hrt[:], 1, False)
                for cc in range(NF):
                    f_transpose_scale(f_hrt[:, 128 * cc:128 * (cc + 1)],
                                      f_hfm[:, cc, 128 * j:128 * (j + 1)],
                                      bcol[:, 32 + cc:32 + cc + 1])
            s.dma("gpsimd", wA[:, :, :], FWq.rearrange("(n p) c -> p n c", p=128))
            for h in range(FH):
                ps = psA0 if h % 2 == 0 else psA1
                mm = [(lambda e, h=h, fc=fc, ps=ps: e.matmul(
                    out=ps[0:64, 0:256], lhsT=wA[:, fc, 64 * h:64 * (h + 1)],
                    rhs=f_hfm[:, fc, :], start=(fc == 0), stop=(fc == NF - 1)),
                    [wA[:, fc, 64 * h:64 * (h + 1)], f_hfm[:, fc, :]])
                    for fc in range(NF)]
                s.mm_group(mm, psA0[0:64, 0:256] if h % 2 == 0 else psA1[0:64, 0:256])
                s.op("vector", lambda e, h=h, ps=ps: e.tensor_scalar(
                    out=f_qpt[:, h, :], in0=ps[0:64, 0:256],
                    scalar1=f_bcolh[:, h:h + 1], scalar2=None, op0=ALU.add),
                    ins=[ps[0:64, 0:256], f_bcolh[:, h:h + 1]], outs=[f_qpt[:, h, :]])
            mm = [(lambda e, fc=fc: e.matmul(
                out=psA0[0:64, 0:256], lhsT=f_wdown[:, fc, :], rhs=f_hfm[:, fc, :],
                start=(fc == 0), stop=(fc == NF - 1)),
                [f_wdown[:, fc, :], f_hfm[:, fc, :]]) for fc in range(NF)]
            s.mm_group(mm, psA0[0:64, 0:256])
            s.op("vector", lambda e: e.tensor_scalar(
                out=f_z, in0=psA0[0:64, 0:256], scalar1=f_bdcol[:, 0:1],
                scalar2=None, op0=ALU.add),
                ins=[psA0[0:64, 0:256], f_bdcol[:, 0:1]], outs=[f_z])
            for g in range(FH):
                ps = psA0 if g % 2 == 0 else psA1
                s.mm_group([(lambda e, g=g, ps=ps: e.matmul(
                    out=ps[0:64, 0:256], lhsT=f_wupk[:, 64 * g:64 * (g + 1)],
                    rhs=f_z, start=True, stop=True),
                    [f_wupk[:, 64 * g:64 * (g + 1)], f_z])], ps[0:64, 0:256])
                s.op("vector", lambda e, g=g, ps=ps: e.tensor_scalar(
                    out=f_kpt[:, g, :], in0=ps[0:64, 0:256],
                    scalar1=f_bcolh2[:, g:g + 1], scalar2=None, op0=ALU.add),
                    ins=[ps[0:64, 0:256], f_bcolh2[:, g:g + 1]], outs=[f_kpt[:, g, :]])
            for j, i in enumerate(tiles):
                for vg in range(2):
                    ps = psA0 if vg == 0 else psA1
                    mm = [(lambda e, j=j, vg=vg, ps=ps: e.matmul(
                        out=ps[:], lhsT=f_zz[:, 128 * j:128 * (j + 1)],
                        rhs=f_wupvb[:, 512 * vg:512 * (vg + 1)],
                        start=True, stop=True),
                        [f_zz[:, 128 * j:128 * (j + 1)],
                         f_wupvb[:, 512 * vg:512 * (vg + 1)]])]
                    s.mm_group(mm, ps[:])
                    s.op("vector", lambda e, vg=vg, ps=ps: e.tensor_copy(
                        out=vsh[:, 0, 512 * vg:512 * (vg + 1)], in_=ps[:]),
                        ins=[ps[:]], outs=[vsh[:, 0, 512 * vg:512 * (vg + 1)]])
                s.dma("sync", FV[:, i, :], vsh[:, 0, :])

            for j, i in enumerate(tiles):
                st0 = ct0 + 128 * j
                for l in range(4):
                    dst = f_vtp.rearrange("p (g w) -> p g w", w=65)[
                        32 * l:32 * l + 16, :, 0:64]
                    srcv = FSCR.rearrange("t (g d) -> t g d", d=64)[
                        st0 + l:st0 + 128:4, :, :].transpose([1, 0, 2])
                    s.dma("sync", dst, srcv)
                for bnk in range(2):
                    ps = psA0 if bnk == 0 else psA1
                    for tt in range(64):
                        t = 64 * bnk + tt
                        lane = t % 4
                        gidx = tt // 4
                        lhsT = f_kpt[:, :, 128 * j + t]
                        rhs = f_qpt[:, :, 128 * j + t]
                        dst = ps[32 * lane:32 * lane + 16, 16 * gidx:16 * (gidx + 1)]
                        s._record("tensor", lambda e, lhsT=lhsT, rhs=rhs, dst=dst,
                                  lane=lane: e.matmul(
                                      out=dst, lhsT=lhsT, rhs=rhs, start=True,
                                      stop=True, tile_position=(0, 32 * lane)),
                                  [lhsT, rhs], [ps[:]], "op")
                    s.op("scalar", lambda e, ps=ps: e.activation(
                        out=f_esb, in_=ps[:], func=AF.Exp, scale=INV_A),
                        ins=[ps[:]], outs=[f_esb])
                    pvps = psB[:, 0:2048]
                    pvcol = lambda gidx: 512 * (gidx // 7) + 65 * (gidx % 7)
                    for tt in range(64):
                        t = 64 * bnk + tt
                        lane = t % 4
                        g_all = t // 4
                        gidx = tt // 4
                        lhsT = f_esb[32 * lane:32 * lane + 16,
                                     16 * gidx:16 * gidx + 16]
                        rhs = f_vtp[32 * lane:32 * lane + 16,
                                    65 * g_all:65 * g_all + 65]
                        dst = pvps[32 * lane:32 * lane + 16,
                                   pvcol(gidx):pvcol(gidx) + 65]
                        s._record("tensor", lambda e, lhsT=lhsT, rhs=rhs, dst=dst,
                                  lane=lane: e.matmul(
                                      out=dst, lhsT=lhsT, rhs=rhs, start=True,
                                      stop=True, tile_position=(32 * lane, 32 * lane)),
                                  [lhsT, rhs], [dst], "op")
                    for br, ng in ((0, 7), (1, 7), (2, 2)):
                        gbase = 7 * br
                        zv = psB[:, 512 * br:512 * br + ng * 65].rearrange(
                            "p (g w) -> p g w", w=65)[:, :, 64:65]
                        rzc = small[:, 16 + gbase:16 + gbase + ng].rearrange(
                            "p (g o) -> p g o", o=1)
                        s.op("vector", lambda e, zv=zv, rzc=rzc: e.reciprocal(
                            out=rzc, in_=zv), ins=[pvps], outs=[small[:, 16:32]])
                        pvv = psB[:, 512 * br:512 * br + ng * 65].rearrange(
                            "p (g w) -> p g w", w=65)[:, :, 0:64]
                        rz = rzc.broadcast_to([128, ng, 64])
                        dstn = f_pvn[:, 1024 * bnk + 64 * gbase:
                                     1024 * bnk + 64 * (gbase + ng)].rearrange(
                            "p (g d) -> p g d", d=64)
                        s.op("vector", lambda e, pvv=pvv, rz=rz, dstn=dstn:
                             e.tensor_tensor(out=dstn, in0=pvv, in1=rz, op=ALU.mult),
                             ins=[pvps, small[:, 16:32]],
                             outs=[f_pvn[:, 1024 * bnk:1024 * (bnk + 1)]])
                for l in range(4):
                    sv = f_pvn.rearrange("p (g d) -> p g d", d=64)[
                        32 * l:32 * l + 16, :, :]
                    dv = FSCR.rearrange("t (h d) -> t h d", d=64)[
                        st0 + l:st0 + 128:4, :, :].transpose([1, 0, 2])
                    s.dma("sync", dv, sv)

            for cc in range(NF):
                s.dma_t("sync", f_aofm[:, cc, :],
                        FSCR[ct0:ct0 + 256, 128 * cc:128 * (cc + 1)])
            s.dma("gpsimd", wA[:, :, :], FWo.rearrange("(n p) c -> p n c", p=128))
            for j, i in enumerate(tiles):
                for vg in range(2):
                    ps = psA0 if vg == 0 else psA1
                    mm = [(lambda e, j=j, vg=vg, fc=fc, ps=ps: e.matmul(
                        out=ps[:], lhsT=f_aofm[:, fc, 128 * j:128 * (j + 1)],
                        rhs=wA[:, fc, 512 * vg:512 * (vg + 1)],
                        start=(fc == 0), stop=False),
                        [f_aofm[:, fc, 128 * j:128 * (j + 1)],
                         wA[:, fc, 512 * vg:512 * (vg + 1)]]) for fc in range(NF)]
                    f_bias_pieces(mm, ps, vg, f_bo, 0)
                    s.mm_group(mm, ps[:])
                    s.op("vector", lambda e, j=j, vg=vg, ps=ps: e.tensor_tensor(
                        out=f_vin[:, j, 512 * vg:512 * (vg + 1)],
                        in0=f_u[:, j, 512 * vg:512 * (vg + 1)], in1=ps[:],
                        op=ALU.add),
                        ins=[f_u[:, j, 512 * vg:512 * (vg + 1)], ps[:]],
                        outs=[f_vin[:, j, 512 * vg:512 * (vg + 1)]])
            for j, i in enumerate(tiles):
                norm_tile(f_vin[:, j, :], f_hrt[:], 1, False)
                for cc in range(NF):
                    f_transpose_scale(f_hrt[:, 128 * cc:128 * (cc + 1)],
                                      f_h2fm[:, cc, 128 * i:128 * (i + 1)],
                                      bcol[:, 40 + cc:40 + cc + 1])
                mm = [(lambda e, i=i, fc=fc: e.matmul(
                    out=psA0[:, 0:FE], lhsT=f_h2fm[:, fc, 128 * i:128 * (i + 1)],
                    rhs=f_wg[:, fc, :], start=(fc == 0), stop=False),
                    [f_h2fm[:, fc, 128 * i:128 * (i + 1)], f_wg[:, fc, :]])
                    for fc in range(NF)]
                mm.append((lambda e: e.matmul(
                    out=psA0[:, 0:FE], lhsT=onesb[0:1, :], rhs=f_bg[0:1, 0:FE],
                    start=False, stop=True), [onesb[0:1, :], f_bg[0:1, 0:FE]]))
                s.mm_group(mm, psA0[:, 0:FE])
                s.op("vector", lambda e: e.tensor_reduce(
                    out=small[:, 40:41], in_=psA0[:, 0:FE], axis=AX.X, op=ALU.max,
                    negate=True), ins=[psA0[:, 0:FE]], outs=[small[:, 40:41]])
                s.op("scalar", lambda e: e.activation(
                    out=small[:, 48:56], in_=psA0[:, 0:FE], func=AF.Exp,
                    bias=small[:, 40:41], accum_out=small[:, 41:42]),
                    ins=[psA0[:, 0:FE], small[:, 40:41]],
                    outs=[small[:, 48:56], small[:, 41:42]])
                s.op("vector", lambda e: e.reciprocal(
                    out=small[:, 42:43], in_=small[:, 41:42]),
                    ins=[small[:, 41:42]], outs=[small[:, 42:43]])
                gdst = f_gate[:, 8 * i:8 * (i + 1)]
                s.op("vector", lambda e, gdst=gdst: e.tensor_scalar(
                    out=gdst, in0=small[:, 48:56], scalar1=small[:, 42:43],
                    scalar2=None, op0=ALU.mult),
                    ins=[small[:, 48:56], small[:, 42:43]], outs=[gdst])
                s.op("vector", lambda e, i=i, j=j: e.tensor_tensor(
                    out=xt[:, i, :], in0=xt[:, i, :], in1=f_vin[:, j, :], op=ALU.add),
                    ins=[xt[:, i, :], f_vin[:, j, :]], outs=[xt[:, i, :]])

        for ex in range(FE):
            s.dma("gpsimd", wA[:, :, :],
                  FWe.rearrange("(eb p) c -> p eb c", p=128).rearrange(
                      "p (e n) c -> p e n c", n=8)[:, ex, :, :])
            for i in range(NT):
                for vg in range(2):
                    ps = psA0 if vg == 0 else psA1
                    mm = [(lambda e, i=i, vg=vg, fc=fc, ps=ps: e.matmul(
                        out=ps[:], lhsT=f_h2fm[:, fc, 128 * i:128 * (i + 1)],
                        rhs=wA[:, fc, 512 * vg:512 * (vg + 1)],
                        start=(fc == 0), stop=(fc == NF - 1)),
                        [f_h2fm[:, fc, 128 * i:128 * (i + 1)],
                         wA[:, fc, 512 * vg:512 * (vg + 1)]]) for fc in range(NF)]
                    s.mm_group(mm, ps[:])
                    xs = xt[:, i, 512 * vg:512 * (vg + 1)]
                    gcol = f_gate[:, 8 * i + ex:8 * i + ex + 1]
                    s.op("vector", lambda e, ps=ps, xs=xs, gcol=gcol:
                         e.scalar_tensor_tensor(out=xs, in0=ps[:], scalar=gcol,
                                                in1=xs, op0=ALU.mult, op1=ALU.add),
                         ins=[ps[:], gcol, xs], outs=[xs])

    qfm_flat = qfm[:].rearrange("p c t -> p (c t)")
    kfm_flat = kfm[:].rearrange("p c t -> p (c t)")
    kfmF = kfm_flat.bitcast(F32)
    gtm_flat = gtm[:].rearrange("p n d -> p (n d)")
    arenaWf = arenaW[:].bitcast(F32)
    arenaAF = arenaA[:].bitcast(F32)

    m_ufm = arenaA[:, 0:16384].rearrange("p (c t) -> p c t", t=T)
    m_u0pad = qfm_flat[:, 0:1048].rearrange("p (c w) -> p c w", w=131)
    m_dt = small[:, 0:512].rearrange("p (n h) -> p n h", h=MH)
    m_dte = small[:, 512:544]
    m_dtbrow = small[0:1, 704:736]
    m_convb = kfmF[0:1, 4096:4096 + M_CONVD]
    m_MT = arenaA[:, 0:4096].rearrange("p (h s) -> p h s", s=MCH)
    m_yc = arenaA[:, 4096:6144]
    m_hsD = arenaA[:, 6144:8192].rearrange("p (h d) -> p h d", d=MP)
    m_Sb = arenaA[:, 8192:10240].rearrange("p (h d) -> p h d", d=MP)
    m_junk = arenaA[:, 10240:12288]
    m_nw = arenaAF[:, 6144:8192]
    m_hs = gtm_flat[:, 0:2048]
    m_hsdt = gtm_flat[:, 2048:4096].rearrange("p (h d) -> p h d", d=MP)
    m_hsdd = gtm_flat[:, 4096:6144]
    m_sg = gtm_flat[:, 6144:8192]
    m_Btm = kfm_flat[:, 12800:12928]
    m_Ctm = kfm_flat[:, 12928:13056]
    m_Bfm = kfm_flat[:, 13056:13184]
    m_Cfm = kfm_flat[:, 13184:13312]
    m_GmT = kfm_flat[:, 13312:13440]
    m_cwb = [kfm_flat[:, 13440:13952], kfm_flat[:, 13952:14464]]
    m_S = qkbuf[:, 768:2816].rearrange("p (h d) -> p h d", d=MP)
    m_macum = qkbuf[:, 0:32]
    m_expA = qkbuf[:, 32:64]
    m_dec = qkbuf[:, 64:96]
    m_Adt = qkbuf[:, 96:128]
    m_AcT = qkbuf[0:32, 128:256]
    m_totr = qkbuf[0:1, 256:288]
    m_etot = qkbuf[:, 288:320]
    m_decp = qkbuf[:, 320:352]
    m_idf = prow[:, 0:128]
    m_triu = prow[:, 128:256]
    m_ealog = prow[:, 256:288]
    m_Dbc = prow[:, 288:320]
    HSV = HSCR.rearrange("(n p) c -> p n c", p=128)
    GSV = GSCR.rearrange("(n p) c -> p n c", p=128)
    BCV = BCSCR.rearrange("(n p) c -> p n c", p=128)
    YSV = YSCR.rearrange("(n p) c -> p n c", p=128)
    FSV = FSCR.rearrange("(n p) d -> p n d", p=128)
    HBW = (512, 512, 512, 512, 256)

    def mamba():
        s.dma("sync", m_idf, CIdentF)
        s.dma("sync", m_triu, CTriuF)
        s.dma("sync", m_dtbrow, MDtB)
        s.dma("sync", m_convb, MConvBRow)
        s.dma("sync", m_Dbc[:, 0:32], MDBc)
        s.dma("sync", qkbuf[:, 352:384], MAlogBc)
        s.op("scalar", lambda e: e.activation(out=m_ealog, in_=qkbuf[:, 352:384],
                                              func=AF.Exp),
             ins=[qkbuf[:, 352:384]], outs=[m_ealog])

        for i in range(NT):
            norm_tile(xt[:, i, :], bsc_pv[:, 0:1024], 0, True)
            s.dma("sync", FSV[:, i, :], bsc_pv[:, 0:1024])
        u0tmp = qfm_flat[:, 1536:2560].rearrange("p (c w) -> p c w", w=128)
        for fc in range(NF):
            s.dma_t("sync", m_ufm[:, fc, :], FSCR[:, 128 * fc:128 * (fc + 1)])
            s.dma_t("sync", u0tmp[:, fc, :], FSCR[0:128, 128 * fc:128 * (fc + 1)])
        s.op("vector", lambda e: e.tensor_copy(out=m_u0pad[:, :, 3:131], in_=u0tmp),
             ins=[u0tmp], outs=[m_u0pad[:, :, 3:131]])
        s.op("vector", lambda e: e.memset(m_u0pad[:, :, 0:3], 0.0),
             outs=[m_u0pad[:, :, 0:3]])

        for g in range(5):
            w = HBW[g]
            wsrc = MWin.rearrange("(n p) c -> p n c", p=128)[
                :, :, M_INTER + 512 * g:M_INTER + 512 * g + w]
            wbuf = arenaWf[:, 0:8 * w].rearrange("p (c w) -> p c w", w=w)
            s.dma("sync", wbuf, wsrc)
            for k in range(MK):
                cw = m_cwb[k % 2]
                s.dma("gpsimd", cw[:, 0:w],
                      MConvWBc[:, M_CONVD * k + 512 * g:M_CONVD * k + 512 * g + w])
                dstw = kfm_flat[:, 4096 * (k % 2):4096 * (k % 2) + 8 * w].rearrange(
                    "p (c w) -> p c w", w=w)
                cwb = cw[:, 0:w].unsqueeze(1).broadcast_to([128, 8, w])
                s.op("vector", lambda e, wbuf=wbuf, cwb=cwb, dstw=dstw: e.tensor_tensor(
                    out=dstw, in0=wbuf, in1=cwb, op=ALU.mult),
                    ins=[wbuf, cw[:, 0:w]], outs=[dstw])
                s.dma("sync", W4SCR.rearrange("(n p) c -> p n c", p=128)[
                    :, :, M_CONVD * k + 512 * g:M_CONVD * k + 512 * g + w], dstw)

        wAh = [arenaW[:, 0:4096].rearrange("p (c w) -> p c w", w=512),
               arenaW[:, 4096:8192].rearrange("p (c w) -> p c w", w=512)]
        sg_stage = kfm_flat[:, 0:2048].rearrange("p (n w) -> p n w", w=512)
        for g4 in range(4):
            wa = wAh[g4 % 2]
            s.dma("gpsimd", wa, MWin.rearrange("(n p) c -> p n c", p=128)[
                :, :, 512 * g4:512 * (g4 + 1)])
            for i in range(NT):
                ps = psA0 if i % 2 == 0 else psA1
                mm = [(lambda e, i=i, fc=fc, ps=ps, wa=wa: e.matmul(
                    out=ps[:], lhsT=m_ufm[:, fc, 128 * i:128 * (i + 1)],
                    rhs=wa[:, fc, :], start=(fc == 0), stop=(fc == NF - 1)),
                    [m_ufm[:, fc, 128 * i:128 * (i + 1)], wa[:, fc, :]])
                    for fc in range(NF)]
                s.mm_group(mm, ps[:])
                dst = sg_stage[:, i % 4, :]
                s.op("scalar", lambda e, ps=ps, dst=dst: e.activation(
                    out=dst, in_=ps[:], func=AF.Silu),
                    ins=[ps[:]], outs=[dst])
                if i % 4 == 3:
                    s.dma("sync", GSV[:, i - 3:i + 1, 512 * g4:512 * (g4 + 1)],
                          sg_stage[:])

        dtw = qfm_flat[:, 1048:1304].rearrange("p (c w) -> p c w", w=32)
        s.dma("gpsimd", dtw, MWin.rearrange("(n p) c -> p n c", p=128)[
            :, :, M_PROJ - MH:M_PROJ])
        for i in range(NT):
            ps = psA0 if i % 2 == 0 else psA1
            mm = [(lambda e, i=i, fc=fc, ps=ps: e.matmul(
                out=ps[:, 0:MH], lhsT=m_ufm[:, fc, 128 * i:128 * (i + 1)],
                rhs=dtw[:, fc, :], start=(fc == 0), stop=False),
                [m_ufm[:, fc, 128 * i:128 * (i + 1)], dtw[:, fc, :]])
                for fc in range(NF)]
            mm.append((lambda e, ps=ps: e.matmul(
                out=ps[:, 0:MH], lhsT=onesf[0:1, :], rhs=m_dtbrow,
                start=False, stop=True), [onesf[0:1, :], m_dtbrow]))
            s.mm_group(mm, ps[:, 0:MH])
            s.op("scalar", lambda e, ps=ps: e.activation(
                out=m_dte, in_=ps[:, 0:MH], func=AF.Exp),
                ins=[ps[:, 0:MH]], outs=[m_dte])
            s.op("scalar", lambda e, i=i: e.activation(
                out=m_dt[:, i, :], in_=m_dte, func=AF.Ln, bias=1.0),
                ins=[m_dte], outs=[m_dt[:, i, :]])

        tapb = [arenaW[:, 0:4096], arenaW[:, 4096:8192],
                gtm_flat[:, 0:4096], gtm_flat[:, 4096:8192]]
        hs_stage = kfm_flat[:, 2048:4096]
        for g in range(5):
            w = HBW[g]
            tbs = [tapb[k][:, 0:8 * w].rearrange("p (c w) -> p c w", w=w)
                   for k in range(MK)]
            for k in range(MK):
                s.dma("sync", tbs[k], W4SCR.rearrange("(n p) c -> p n c", p=128)[
                    :, :, M_CONVD * k + 512 * g:M_CONVD * k + 512 * g + w])
            cbr = m_convb[:, 512 * g:512 * g + w]
            for i in range(NT):
                ps = psA0 if i % 2 == 0 else psA1
                mm = []
                for k in range(MK):
                    for fc in range(NF):
                        if i == 0:
                            lhsT = m_u0pad[:, fc, k:k + 128]
                        else:
                            a = 128 * i - 3 + k
                            lhsT = m_ufm[:, fc, a:a + 128]
                        mm.append((lambda e, lhsT=lhsT, fc=fc, k=k, ps=ps, w=w,
                                   tb=tbs[k]: e.matmul(
                            out=ps[:, 0:w], lhsT=lhsT, rhs=tb[:, fc, :],
                            start=(k == 0 and fc == 0), stop=False),
                            [lhsT, tbs[k][:, fc, :]]))
                mm.append((lambda e, ps=ps, w=w, cbr=cbr: e.matmul(
                    out=ps[:, 0:w], lhsT=onesf[0:1, :], rhs=cbr,
                    start=False, stop=True), [onesf[0:1, :], cbr]))
                s.mm_group(mm, ps[:, 0:w])
                dst = hs_stage[:, (i % 4) * w:(i % 4 + 1) * w]
                s.op("scalar", lambda e, ps=ps, dst=dst, w=w: e.activation(
                    out=dst, in_=ps[:, 0:w], func=AF.Silu),
                    ins=[ps[:, 0:w]], outs=[dst])
                if i % 4 == 3:
                    src4 = hs_stage[:, 0:4 * w].rearrange("p (n w) -> p n w", w=w)
                    if g < 4:
                        s.dma("sync", HSV[:, i - 3:i + 1, 512 * g:512 * g + w], src4)
                    else:
                        s.dma("sync", BCV[:, i - 3:i + 1, :], src4)

        s.op("vector", lambda e: e.memset(qkbuf[:, 768:2816], 0.0),
             outs=[qkbuf[:, 768:2816]])
        s.dma("gpsimd", bsc_pv[:, 0:2048], CMaskW)
        s.dma("sync", m_nw, MNwBc)
        for c in range(MNC):
            dtc = m_dt[:, c, :]
            s.dma("sync", m_hs, HSV[:, c, :])
            s.dma("sync", kfm_flat[:, 12800:13056], BCV[:, c, :])
            s.dma("sync", m_sg, GSV[:, c, :])
            s.op("vector", lambda e, dtc=dtc: e.scalar_tensor_tensor(
                out=m_Adt, in0=dtc, scalar=-1.0, in1=m_ealog,
                op0=ALU.mult, op1=ALU.mult),
                ins=[dtc, m_ealog], outs=[m_Adt])
            s.mm_group([(lambda e: e.matmul(out=psA0[:, 0:32], lhsT=m_triu,
                                            rhs=m_Adt, start=True, stop=True),
                         [m_triu, m_Adt])], psA0[:, 0:32])
            s.mm_group([(lambda e: e.matmul(out=psA0[0:1, 32:64], lhsT=onesf[:, 0:1],
                                            rhs=m_Adt, start=True, stop=True),
                         [onesf[:, 0:1], m_Adt])], psA0[0:1, 32:64])
            s.op("vector", lambda e: e.tensor_copy(out=m_macum, in_=psA0[:, 0:32]),
                 ins=[psA0[:, 0:32]], outs=[m_macum])
            s.op("vector", lambda e: e.tensor_copy(out=m_totr, in_=psA0[0:1, 32:64]),
                 ins=[psA0[0:1, 32:64]], outs=[m_totr])
            s.op("scalar", lambda e: e.activation(out=m_expA, in_=m_macum, func=AF.Exp),
                 ins=[m_macum], outs=[m_expA])
            s.mm_group([(lambda e: e.matmul(out=psA0[:, 64:96], lhsT=onesf[0:1, :],
                                            rhs=m_totr, start=True, stop=True),
                         [onesf[0:1, :], m_totr])], psA0[:, 64:96])
            s.op("scalar", lambda e: e.activation(out=m_etot, in_=psA0[:, 64:96],
                                                  func=AF.Exp),
                 ins=[psA0[:, 64:96]], outs=[m_etot])
            s.op("vector", lambda e: e.tensor_tensor(out=m_decp, in0=psA0[:, 64:96],
                                                     in1=m_macum, op=ALU.subtract),
                 ins=[psA0[:, 64:96], m_macum], outs=[m_decp])
            s.op("scalar", lambda e: e.activation(out=m_dec, in_=m_decp, func=AF.Exp),
                 ins=[m_decp], outs=[m_dec])
            s.mm_group([(lambda e: e.matmul(out=psA1[0:32, 0:128], lhsT=m_macum,
                                            rhs=m_idf, start=True, stop=True),
                         [m_macum, m_idf])], psA1[0:32, 0:128])
            s.op("vector", lambda e: e.tensor_copy(out=m_AcT, in_=psA1[0:32, 0:128]),
                 ins=[psA1[0:32, 0:128]], outs=[m_AcT])
            s.dma("sync", arenaWf[0:1, 0:2048], qkbuf[0:16, 128:256])
            s.dma("sync", arenaWf[0:1, 2048:4096], qkbuf[16:32, 128:256])
            for hh in range(2):
                arow = arenaWf[0:1, 2048 * hh:2048 * hh + 2048]
                ones1 = onesf[0:1, :]
                mm = []
                for b in range(4):
                    mm.append((lambda e, b=b, arow=arow, ones1=ones1: e.matmul(
                        out=psB[:, 512 * b:512 * (b + 1)], lhsT=ones1,
                        rhs=arow[:, 512 * b:512 * (b + 1)], start=True, stop=False),
                        [ones1, arow[:, 512 * b:512 * (b + 1)]]))
                    mm.append((lambda e, b=b: e.matmul(
                        out=psB[:, 512 * b:512 * (b + 1)], lhsT=cident[:],
                        rhs=bsc_pv[:, 512 * b:512 * (b + 1)], start=False, stop=True),
                        [cident[:], bsc_pv[:, 512 * b:512 * (b + 1)]]))
                s.mm_group(mm, psB[:])
                acz = m_macum[:, 16 * hh:16 * (hh + 1)].unsqueeze(2).broadcast_to(
                    [128, 16, 128])
                psv = psB[:].rearrange("p (h s) -> p h s", s=128)
                s.op("vector", lambda e, psv=psv, acz=acz: e.tensor_tensor(
                    out=psv, in0=psv, in1=acz, op=ALU.subtract),
                    ins=[psB[:], m_macum[:, 16 * hh:16 * (hh + 1)]], outs=[psB[:]])
                mth = arenaA[:, 2048 * hh:2048 * (hh + 1)]
                s.op("scalar", lambda e, mth=mth: e.activation(
                    out=mth, in_=psB[:], func=AF.Exp), ins=[psB[:]], outs=[mth])
            transpose128(m_Btm, m_Bfm)
            transpose128(m_Ctm, m_Cfm)
            s.mm_group([(lambda e: e.matmul(out=psA0[:, 128:256], lhsT=m_Bfm,
                                            rhs=m_Cfm, start=True, stop=True),
                         [m_Bfm, m_Cfm])], psA0[:, 128:256])
            s.op("vector", lambda e: e.tensor_copy(out=m_GmT, in_=psA0[:, 128:256]),
                 ins=[psA0[:, 128:256]], outs=[m_GmT])
            for hh in range(2):
                mth = arenaA[:, 2048 * hh:2048 * (hh + 1)].rearrange(
                    "p (h s) -> p h s", s=128)
                gmb = m_GmT.unsqueeze(1).broadcast_to([128, 16, 128])
                s.op("vector", lambda e, mth=mth, gmb=gmb: e.tensor_tensor(
                    out=mth, in0=mth, in1=gmb, op=ALU.mult),
                    ins=[arenaA[:, 2048 * hh:2048 * (hh + 1)], m_GmT],
                    outs=[arenaA[:, 2048 * hh:2048 * (hh + 1)]])
            hsv = m_hs.rearrange("p (h d) -> p h d", d=MP)
            dtb = dtc.unsqueeze(2).broadcast_to([128, MH, MP])
            s.op("vector", lambda e, hsv=hsv, dtb=dtb: e.tensor_tensor(
                out=m_hsdt, in0=hsv, in1=dtb, op=ALU.mult),
                ins=[m_hs, dtc], outs=[gtm_flat[:, 2048:4096]])
            decb = m_dec.unsqueeze(2).broadcast_to([128, MH, MP])
            hddv = m_hsdd.rearrange("p (h d) -> p h d", d=MP)
            s.op("vector", lambda e, hddv=hddv, decb=decb: e.tensor_tensor(
                out=hddv, in0=m_hsdt, in1=decb, op=ALU.mult),
                ins=[gtm_flat[:, 2048:4096], m_dec], outs=[m_hsdd])
            dbb = m_Dbc[:, 0:32].unsqueeze(2).broadcast_to([128, MH, MP])
            s.op("vector", lambda e, hsv=hsv, dbb=dbb: e.tensor_tensor(
                out=m_hsD, in0=hsv, in1=dbb, op=ALU.mult),
                ins=[m_hs, m_Dbc[:, 0:32]], outs=[arenaA[:, 6144:8192]])
            s.op("scalar", lambda e: e.activation(
                out=arenaA[:, 8192:10240], in_=qkbuf[:, 768:2816],
                func=AF.Copy),
                ins=[qkbuf[:, 768:2816]], outs=[arenaA[:, 8192:10240]])
            for h in range(MH):
                mm = [(lambda e, h=h: e.matmul(
                    out=psB[:, MP * h:MP * (h + 1)], lhsT=m_MT[:, h, :],
                    rhs=m_hsdt[:, h, :], start=True, stop=False),
                    [m_MT[:, h, :], m_hsdt[:, h, :]]),
                    (lambda e, h=h: e.matmul(
                        out=psB[:, MP * h:MP * (h + 1)], lhsT=cident[:],
                        rhs=m_hsD[:, h, :], start=False, stop=True),
                        [cident[:], m_hsD[:, h, :]])]
                s.mm_group(mm, psB[:, MP * h:MP * (h + 1)])
            for hh in range(2):
                for hp in range(16):
                    h = 16 * hh + hp
                    ps = psA0 if hp < 8 else psA1
                    dst = ps[:, MP * (hp % 8):MP * (hp % 8 + 1)]
                    s.mm_group([(lambda e, h=h, dst=dst: e.matmul(
                        out=dst, lhsT=m_Cfm, rhs=m_Sb[:, h, :],
                        start=True, stop=True), [m_Cfm, m_Sb[:, h, :]])], dst)
                yh = m_yc[:, 1024 * hh:1024 * (hh + 1)]
                pbh = psB[:, 1024 * hh:1024 * (hh + 1)]
                s.op("vector", lambda e, yh=yh, pbh=pbh: e.tensor_copy(
                    out=yh, in_=pbh), ins=[pbh], outs=[yh])
                for hp in range(16):
                    h = 16 * hh + hp
                    ps = psA0 if hp < 8 else psA1
                    src = ps[:, MP * (hp % 8):MP * (hp % 8 + 1)]
                    ydst = m_yc[:, MP * h:MP * (h + 1)]
                    s.op("vector", lambda e, src=src, ydst=ydst, h=h:
                         e.scalar_tensor_tensor(
                             out=ydst, in0=src, scalar=m_expA[:, h:h + 1],
                             in1=ydst, op0=ALU.mult, op1=ALU.add),
                         ins=[src, m_expA[:, h:h + 1], ydst],
                         outs=[ydst])
            for b in range(4):
                s.mm_group([(lambda e, b=b: e.matmul(
                    out=psB[:, 512 * b:512 * (b + 1)], lhsT=m_Btm,
                    rhs=m_hsdd[:, 512 * b:512 * (b + 1)], start=True, stop=True),
                    [m_Btm, m_hsdd[:, 512 * b:512 * (b + 1)]])],
                    psB[:, 512 * b:512 * (b + 1)])
            etb = m_etot.unsqueeze(2).broadcast_to([128, MH, MP])
            s.op("vector", lambda e, etb=etb: e.tensor_tensor(
                out=m_S, in0=m_S, in1=etb, op=ALU.mult),
                ins=[qkbuf[:, 768:2816], m_etot], outs=[qkbuf[:, 768:2816]])
            s.op("vector", lambda e: e.tensor_tensor(
                out=qkbuf[:, 768:2816], in0=qkbuf[:, 768:2816], in1=psB[:],
                op=ALU.add),
                ins=[qkbuf[:, 768:2816], psB[:]], outs=[qkbuf[:, 768:2816]])
            s.op("vector", lambda e: e.tensor_tensor(
                out=m_yc, in0=m_yc, in1=m_sg, op=ALU.mult),
                ins=[arenaA[:, 4096:6144], m_sg], outs=[arenaA[:, 4096:6144]])
            cst = lambda j: small[:, 736 + j:736 + j + 1]
            s.op("scalar", lambda e: e.activation(out=m_junk, in_=m_yc, func=AF.Square,
                                                  accum_out=cst(0)),
                 ins=[arenaA[:, 4096:6144]], outs=[m_junk, cst(0)])
            s.op("vector", lambda e: e.tensor_scalar(
                out=cst(5), in0=cst(0), scalar1=1.0 / M_INTER, scalar2=None,
                op0=ALU.mult), ins=[cst(0)], outs=[cst(5)])
            s.op("scalar", lambda e: e.activation(out=cst(6), in_=cst(5), func=AF.Sqrt,
                                                  bias=consts[:, 1:2]),
                 ins=[cst(5), consts[:, 1:2]], outs=[cst(6)])
            s.op("vector", lambda e: e.reciprocal(out=cst(7), in_=cst(6)),
                 ins=[cst(6)], outs=[cst(7)])
            s.op("vector", lambda e: e.scalar_tensor_tensor(
                out=m_junk, in0=m_yc, scalar=cst(7), in1=m_nw,
                op0=ALU.mult, op1=ALU.mult),
                ins=[arenaA[:, 4096:6144], cst(7), m_nw], outs=[m_junk])
            s.dma("sync", YSV[:, c, :], m_junk)

        for half in range(2):
            yfm = m_ufm
            for j in range(8):
                cc = 8 * half + j
                s.dma_t("sync", yfm[:, j, :], YSCR[:, 128 * cc:128 * (cc + 1)])
            for dg in range(2):
                wa = wAh[(2 * half + dg) % 2]
                s.dma("gpsimd", wa, MWout.rearrange("(n p) d -> p n d", p=128)[
                    :, 8 * half:8 * half + 8, 512 * dg:512 * (dg + 1)])
                for i in range(NT):
                    ps = psA0 if i % 2 == 0 else psA1
                    mm = [(lambda e, i=i, fc=fc, ps=ps, wa=wa: e.matmul(
                        out=ps[:], lhsT=yfm[:, fc, 128 * i:128 * (i + 1)],
                        rhs=wa[:, fc, :], start=(fc == 0), stop=(fc == NF - 1)),
                        [yfm[:, fc, 128 * i:128 * (i + 1)], wa[:, fc, :]])
                        for fc in range(NF)]
                    s.mm_group(mm, ps[:])
                    xs = xt[:, i, 512 * dg:512 * (dg + 1)]
                    s.op("vector", lambda e, xs=xs, ps=ps: e.tensor_tensor(
                        out=xs, in0=xs, in1=ps[:], op=ALU.add),
                        ins=[xs, ps[:]], outs=[xs])

    if STAGE >= 1:
        attn_a()
    if STAGE >= 2:
        ffn(0)
    if STAGE >= 3:
        mamba()
    if STAGE >= 4:
        ffn(1)

    if dbg_outs:
        srcs = {
            "d_lnfm": arenaA[:, 8 * D:16 * D],
            "d_qfm": qfm[:].rearrange("p c t -> p (c t)"),
            "d_kfm": kfm[:].rearrange("p c t -> p (c t)"),
            "d_gtm": gtm[:].rearrange("p n d -> p (n d)"),
            "d_qk": qkbuf[:],
            "d_x": xt[:].rearrange("p n d -> p (n d)"),
            "d_fscr": FSCR,
            "d_h2fm": kfm[:].rearrange("p c t -> p (c t)"),
            "d_gate": qkbuf[:].bitcast(BF16)[:, 2048:2048 + 256].bitcast(F32),
            "d_qpt": gtm[:].rearrange("p n d -> p (n d)")[0:64, 0:8192],
            "d_hscr": HSCR,
            "d_gscr": GSCR,
            "d_bcscr": BCSCR,
            "d_yscr": YSCR,
            "d_qkb2": qkbuf[:],
            "d_mt": arenaA[:, 0:4096],
            "d_mdt": small[:, 0:512],
        }
        for nm, ap in dbg_outs.items():
            s.dma("sync", ap, srcs[nm])

    if TRUNC is not None:
        s.truncate(TRUNC)
    OUTV = OUT.rearrange("(n p) d -> p n d", p=128)
    s.dma("sync", OUTV, xt[:])
    s.op("sync", lambda e: e.nop(), ins=[OUTV])
    s.build()
    return nc, s


def _prep_inputs(inputs):
    g = lambda n: np.asarray(inputs[n])
    x = g("L0_attn_a_x")
    d = {}
    d["aa_wqkv"] = np.ascontiguousarray(g("L0_attn_a_qkv_w").T)
    qkb = g("L0_attn_a_qkv_b")
    bqk = np.zeros((128, 16), np.float32)
    for j in range(8):
        bqk[:, j] = qkb[128 * j:128 * (j + 1)]
        bqk[:, 8 + j] = qkb[D + 128 * j:D + 128 * (j + 1)]
    d["aa_bqk_col"] = bqk
    def piece_rows(vec, bounds=(0, 352, 704, 1024), width=352):
        qrow = np.zeros((128, width), np.float32)
        for qq in range(3):
            w = bounds[qq + 1] - bounds[qq]
            qrow[32 * qq, :w] = vec[bounds[qq]:bounds[qq + 1]]
        return qrow
    d["aa_bvq"] = qkb[2 * D:3 * D].reshape(1, -1)
    d["aa_wout"] = np.ascontiguousarray(g("L0_attn_a_out_w").T)
    d["aa_boutq"] = g("L0_attn_a_out_b").reshape(1, -1)
    d["aa_wgate"] = np.ascontiguousarray(g("L0_attn_a_gate_w").T)
    d["aa_bgateq"] = g("L0_attn_a_gate_b").reshape(1, -1)
    pb = np.ascontiguousarray(g("L0_attn_a_pos_bias").T).reshape(-1)
    d["aa_posbq"] = piece_rows(pb, (0, 256, 512, 704), 256)
    se = np.ascontiguousarray(g("L0_attn_a_scale_embed").T)
    d["aa_se2"] = np.concatenate([se, se], axis=0)
    d["aa_ifg"] = g("L0_attn_a_if_gain").reshape(1, -1)
    mask = np.zeros((128, 13 * NOFF), np.float32)
    for ti in range(13):
        tglob = 128 * ti + np.arange(128)
        for ii, o in enumerate(OFFSETS):
            mask[:, ti * NOFF + ii] = np.where(tglob >= o, 0.0, NEG)
    d["aa_mask"] = mask
    d["f_wq"] = g("L0_ffn_Wq")
    d["f_wdown"] = g("L0_ffn_Wdown")
    d["f_bdown_col"] = g("L0_ffn_bdown").reshape(64, 1)
    wup = g("L0_ffn_Wup")
    wupg = wup.reshape(64, 16, 128)
    d["f_wupk"] = np.ascontiguousarray(wupg[:, :, 0:64].reshape(64, 1024))
    d["f_wupv"] = np.ascontiguousarray(wupg[:, :, 64:128].reshape(64, 1024))
    fbup = g("L0_ffn_bup").reshape(16, 128)
    d["f_bupk_colh"] = np.ascontiguousarray(fbup[:, 0:64].T)
    d["f_bupvq"] = np.ascontiguousarray(fbup[:, 64:128].reshape(1, -1))
    fbq = g("L0_ffn_bq")
    d["f_bq_colh"] = np.ascontiguousarray(fbq.reshape(16, 64).T)
    d["f_boq"] = g("L0_ffn_bo").reshape(1, -1)
    d["f_wo"] = g("L0_ffn_Wo")
    d["f_bo"] = g("L0_ffn_bo").reshape(1, -1)
    g1 = g("L0_ffn_gamma1"); g2 = g("L0_ffn_gamma2")
    d["f_g1_col"] = np.stack([g1[128 * j:128 * (j + 1)] for j in range(8)], axis=1)
    d["f_g2_col"] = np.stack([g2[128 * j:128 * (j + 1)] for j in range(8)], axis=1)
    d["f_wg"] = g("L0_ffn_Wg")
    d["f_bg"] = g("L0_ffn_bg").reshape(1, -1)
    d["f_we"] = np.ascontiguousarray(g("L0_ffn_We").reshape(FE * D, D))
    d["m_win"] = np.ascontiguousarray(g("L0_attn_b_in_proj_w").T)
    cw = g("L0_attn_b_conv_w")[:, 0, :]
    d["m_convw_bc"] = np.ascontiguousarray(
        np.tile(cw.T.reshape(1, -1), (128, 1)).astype(np.float32))
    d["m_convb_row"] = g("L0_attn_b_conv_b").reshape(1, -1).astype(np.float32)
    d["m_dtb"] = g("L0_attn_b_dt_bias").reshape(1, -1)
    d["m_alog_bc"] = np.ascontiguousarray(
        np.tile(g("L0_attn_b_A_log").reshape(1, -1), (128, 1)).astype(np.float32))
    d["m_d_bc"] = np.ascontiguousarray(
        np.tile(g("L0_attn_b_D").reshape(1, -1), (128, 1)).astype(np.float32))
    d["m_nw_bc"] = np.ascontiguousarray(
        np.tile(g("L0_attn_b_norm_w").reshape(1, -1), (128, 1)).astype(np.float32))
    d["m_wout"] = np.ascontiguousarray(g("L0_attn_b_out_proj_w").T)
    d["c_ident"] = np.eye(128, dtype=BF)
    seg = np.zeros((128, 2), dtype=BF)
    seg[0:64, 0] = 1
    seg[64:128, 1] = 1
    d["c_seg64"] = seg
    d["c_triuf"] = np.triu(np.ones((128, 128), np.float32))
    d["c_identf"] = np.eye(128, dtype=np.float32)
    zz, ss = np.meshgrid(np.arange(128), np.arange(128), indexing="ij")
    m0 = np.where(ss >= zz, 0.0, -3e38).astype(BF)
    d["c_maskw"] = np.ascontiguousarray(np.tile(m0, (1, 16)))
    in_maps = []
    for core in range(8):
        m = dict(d)
        m["x"] = np.ascontiguousarray(x[core])
        in_maps.append(m)
    return in_maps


def kernel(**inputs):
    in_maps = _prep_inputs(inputs)
    nc, kb = build_program()
    res = run_bass_kernel_spmd(nc, in_maps, core_ids=list(range(8)))
    return np.stack([np.asarray(r["out"], dtype=np.float32) for r in res.results], axis=0)
